# version 01
# baseline (speedup 1.0000x reference)
import numpy as np
import ml_dtypes
import concourse.bass as bass
import concourse.mybir as mybir
from concourse.bass_utils import run_bass_kernel_spmd

F32 = mybir.dt.float32
BF16 = mybir.dt.bfloat16
I32 = mybir.dt.int32
AF = mybir.ActivationFunctionType
ALU = mybir.AluOpType
AX = mybir.AxisListType

T, D, H, DH, HID, NCORE = 8192, 2048, 16, 128, 5632, 8
NJ = D // 128
NHB = HID // 128
NTIL = T // 512
EPS_LN = 1e-5
EPS_Q = 1e-8
SCALE = (DH ** 0.5) / DH
WCNT = float(HID * D)

UPTO = 9
DEBUG_TAPS = ()

OFF = {
    "consts": 0,
    "cosf":   8192,
    "sinf":   24576,
    "xnFo":   8192,
    "yhat":   8192,
    "wkv":    40960,
    "qT":     40960,
    "out":    40960,
    "wqt":    73728,
    "q12":    73728,
    "xo":     81920,
    "sg":     90112,
    "xts":    98304,
    "wstage": 106496,
    "x":      106496,
    "trig":   106496,
    "hT":     106496,
    "xnT":    122880,
    "qscr":   129536,
    "lnscr":  153088,
    "stats":  161280,
    "xnFc":   163840,
    "kvst":   196608,
    "att":    196608,
}
SB_BASE = 16512
SB_LIMIT = 212000

SEMNAMES = ["c_const", "c_wkv", "c_wf0", "c_wf1", "c_x0", "c_x1",
            "c_wq0", "c_wq1", "c_xo0", "c_xo1",
            "c_q0", "c_q1", "c_q2", "c_q3",
            "c_k0", "c_k1", "c_v0", "c_v1",
            "c_w30", "c_w31", "c_wo0", "c_wo1", "c_xt0", "c_xt1",
            "c_kvw", "c_trw", "c_ab", "c_fl", "c_out", "c_xtw", "c_tap",
            "s_pe", "s_act", "s_dve", "s_gp", "p_poll"]

SWITCHED_P7 = ("sy", "pe", "act", "dve")


class Dummy:
    def __getattr__(self, k):
        def f(*a, **kw):
            return self
        return f

    def then_inc(self, *a, **kw):
        return self


class Emitter:
    def __init__(self, nc, me, eng, sems, sb, ps, ap, taps, upto):
        self.nc = nc
        self.me = me
        self.real = eng
        self.dummy = Dummy()
        self.sems = sems
        self.sb = sb
        self.ps = ps
        self.ap = ap
        self.taps = taps
        self.upto = upto
        self.c = dict.fromkeys(SEMNAMES, 0)
        self.cur_w = None

    def h(self, which):
        return self.real if which == self.me else self.dummy

    def wait(self, which, sem, val):
        if which == self.me and val > 0:
            self.real.wait_ge(self.sems[sem], val)

    def dma(self, which, sem, out, in_, **kw):
        self.h(which).dma_start(out=out, in_=in_, **kw).then_inc(
            self.sems[sem], 16)
        self.c[sem] += 16

    def gp_sw_dma(self, sem, dst_fn, src_fn):
        if self.me == "gp":
            if self.cur_w is not None:
                self.real.dma_start(out=dst_fn(self.cur_w),
                                    in_=src_fn(self.cur_w)).then_inc(
                    self.sems[sem], 16)
            else:
                for w in self.real.Switch(self.w_sv, 2):
                    self.real.dma_start(out=dst_fn(w), in_=src_fn(w)).then_inc(
                        self.sems[sem], 16)
        self.c[sem] += 16

    def wswitch(self, body_fn):
        if self.me == "gp":
            entry = dict(self.c)
            end = None
            for w in self.real.Switch(self.w_sv, 2):
                self.c = dict(entry)
                self.cur_w = w
                body_fn(w)
                if end is None:
                    end = dict(self.c)
                else:
                    assert end == self.c, "wswitch body count mismatch"
            self.cur_w = None
            self.c = end
        else:
            body_fn(0)

    def inc(self, instr, sem, n=1):
        instr.then_inc(self.sems[sem], n)
        self.c[sem] += n

    def fence(self, which, instr):
        sem = {"pe": "s_pe", "act": "s_act", "dve": "s_dve",
               "gp": "s_gp", "sy": "s_gp"}[which]
        self.inc(instr, sem)
        self.wait(which, sem, self.c[sem])

    def setup_regs(self):
        me, r = self.me, self.real
        rank_reg = r.alloc_register(f"rank_{me}")
        for arm in r.switch_core_id(n=128):
            r.reg_mov(rank_reg, arm.logical % NCORE)
        self.rank_sv = r.snap(rank_reg, min_val=0, max_val=NCORE - 1)
        if me == "gp":
            w_reg = r.alloc_register("w_gp")
            r.reg_alu(w_reg, rank_reg, 1, op=ALU.bitwise_and)
            self.w_sv = r.snap(w_reg, min_val=0, max_val=1)
            self.pollregs = {
                "nonce": r.alloc_register("po_nonce"),
                "flag": r.alloc_register("po_flag"),
                "diff": r.alloc_register("po_diff"),
                "cnt": r.alloc_register("po_cnt"),
            }
            r.reg_mov(self.pollregs["cnt"], 0)

    def poll(self, slot):
        self.wait("gp", "c_const", self.c["c_const"])
        if self.me == "gp":
            g = self.real
            pr = self.pollregs
            g.reg_load(pr["nonce"], self.sb["noncesb"][0:1, 0:1])
            for w in g.Switch(self.w_sv, 2):
                p = 1 - w

                def poll_once():
                    g.reg_add(pr["cnt"], pr["cnt"], 16)
                    g.dma_start(out=self.sb["flagsb"][0:1, :],
                                in_=self.ap["flags_sh"][p:p + 1, slot, :]
                                ).then_inc(self.sems["p_poll"], 16)
                    g.wait_ge(self.sems["p_poll"], pr["cnt"])
                    g.reg_load(pr["flag"], self.sb["flagsb"][0:1, 0:1])
                    g.reg_alu(pr["diff"], pr["flag"], pr["nonce"],
                              op=ALU.subtract)

                poll_once()
                with g.While(pr["diff"]):
                    poll_once()
            g.sem_inc(self.sems["s_gp"], 1)
        self.c["s_gp"] += 1

    def set_flag(self, slot):
        self.gp_sw_dma("c_fl",
                       lambda w: self.ap["flags_sh"][w:w + 1, slot, :],
                       lambda w: self.sb["noncesb"][0:1, :])

    def p0_consts(self):
        sb, ap = self.sb, self.ap
        self.dma("gp", "c_const", sb["identbf"][:], ap["ident32"][:, :])
        self.dma("gp", "c_const", sb["masks"][:], ap["masks"][:, :])
        self.dma("sy", "c_const", sb["ident32"][:], ap["ident32"][:, :])
        self.dma("sy", "c_const", sb["cosf"][:], ap["cosb"][:, :])
        self.dma("sy", "c_const", sb["sinf"][:], ap["sinb"][:, :])
        self.dma("sy", "c_const", sb["noncesb"][:], ap["nonce"][:, :])
        dve = self.h("dve")
        dve.memset(sb["onesbf"][:], 1.0)
        dve.memset(sb["onesf1"][:], 1.0)
        dve.memset(sb["onescol"][:], 1.0)
        dve.memset(sb["neghalf"][:], -0.5)
        i = dve.memset(sb["stats"][:, 7:8], EPS_LN)
        self.inc(i, "s_dve")

    def p1_wkv(self):
        for j in range(NJ):
            self.dma("gp", "c_wkv", self.sb["wkv"][:, j * 2048:(j + 1) * 2048],
                     self.ap["wkvTb"][j, :, :])

    def p2_gamma(self):
        sb, c = self.sb, self.c
        dve, pe = self.h("dve"), self.h("pe")
        nt = NHB + NJ
        red0 = c["s_dve"]
        for it in range(nt):
            slot = it % 2
            wst = sb["wst0"] if slot == 0 else sb["wst1"]
            csem = "c_wf0" if slot == 0 else "c_wf1"
            if it >= 2:
                self.wait("sy", "s_dve", red0 + it - 1)
            if it < NHB:
                self.dma("sy", csem, wst[:, 0:2048], self.ap["wFb"][it, :, :])
                wid = 2048
            else:
                self.dma("sy", csem, wst[:, 0:2816],
                         self.ap["w3b"][it - NHB, :, :])
                wid = 2816
            self.wait("dve", csem, c[csem])
            i = dve.reduce_sum(sb["lnscr"][:, it:it + 1], wst[:, 0:wid],
                               axis=AX.X, apply_absolute_value=True)
            if it == nt - 1:
                self.fence("dve", i)
            else:
                self.inc(i, "s_dve")
        dve.reduce_sum(sb["lnscr"][:, 60:61], sb["lnscr"][:, 0:NHB], axis=AX.X)
        i = dve.reduce_sum(sb["lnscr"][:, 61:62], sb["lnscr"][:, NHB:NHB + NJ],
                           axis=AX.X)
        self.inc(i, "s_dve")
        self.wait("pe", "s_dve", c["s_dve"])
        i = pe.matmul(self.ps["A"][0:1, 0:2], sb["onescol"][:, 0:1],
                      sb["lnscr"][:, 60:62], start=True, stop=True)
        self.inc(i, "s_pe")
        self.wait("dve", "s_pe", c["s_pe"])
        i = dve.tensor_copy(sb["gamv"][0:1, 0:2], self.ps["A"][0:1, 0:2])
        self.fence("dve", i)
        i = dve.tensor_scalar(sb["gamv"][0:1, 2:3], sb["gamv"][0:1, 0:1],
                              1.0 / 5632.0, 0.0, ALU.mult, ALU.add)
        self.fence("dve", i)
        i = dve.tensor_scalar(sb["gamv"][0:1, 2:3], sb["gamv"][0:1, 2:3],
                              1.0 / 2048.0, EPS_Q, ALU.mult, ALU.add)
        self.fence("dve", i)
        i = dve.reciprocal(sb["gamv"][0:1, 3:4], sb["gamv"][0:1, 2:3])
        self.inc(i, "s_dve")
        self.wait("pe", "s_dve", c["s_dve"])
        i = pe.matmul(self.ps["A"][:, 4:5], sb["onesf1"][0:1, :],
                      sb["gamv"][0:1, 3:4], start=True, stop=True)
        self.inc(i, "s_pe")
        self.wait("dve", "s_pe", c["s_pe"])
        i = dve.tensor_copy(sb["invg"][:, 0:1], self.ps["A"][:, 4:5])
        self.fence("dve", i)
        self.wait("gp", "s_dve", c["s_dve"])
        self.wait("gp", "c_const", c["c_const"])

        def _ab_w(w):
            self.gp_sw_dma("c_ab", lambda w: self.ap["absum_sh"][w:w + 1, :],
                           lambda w: sb["gamv"][0:1, 0:8])
            self.wait("gp", "c_ab", self.c["c_ab"])
            self.set_flag(2)
        self.wswitch(_ab_w)
        self.poll(2)
        self.wswitch(lambda w: self.gp_sw_dma(
            "c_ab", lambda w: sb["absb"][0:1, :],
            lambda w: self.ap["absum_sh"][1 - w:2 - w, :]))
        self.wait("dve", "c_ab", c["c_ab"])
        self.wait("dve", "s_gp", c["s_gp"])
        i = dve.tensor_add(sb["gamv"][0:1, 4:5], sb["gamv"][0:1, 1:2],
                           sb["absb"][0:1, 1:2])
        self.fence("dve", i)
        i = dve.tensor_scalar(sb["gamv"][0:1, 5:6], sb["gamv"][0:1, 4:5],
                              1.0 / 5632.0, 0.0, ALU.mult, ALU.add)
        self.fence("dve", i)
        i = dve.tensor_scalar(sb["gamv"][0:1, 5:6], sb["gamv"][0:1, 5:6],
                              1.0 / 2048.0, EPS_Q, ALU.mult, ALU.add)
        self.fence("dve", i)
        i = dve.reciprocal(sb["gamv"][0:1, 6:7], sb["gamv"][0:1, 5:6])
        self.inc(i, "s_dve")
        self.wait("pe", "s_dve", c["s_dve"])
        i = pe.matmul(self.ps["A"][:, 5:6], sb["onesf1"][0:1, :],
                      sb["gamv"][0:1, 6:7], start=True, stop=True)
        self.inc(i, "s_pe")
        self.wait("dve", "s_pe", c["s_pe"])
        i = dve.tensor_copy(sb["invg"][:, 1:2], self.ps["A"][:, 5:6])
        self.fence("dve", i)
        if "gam" in self.taps:
            self.wait("gp", "s_dve", c["s_dve"])
            self.dma("gp", "c_tap", self.ap["tap_gam"][:, :], sb["gamv"][0:1, :])

    def p3_quantize(self):
        def body(w):
            sb, c = self.sb, self.c
            dve, act = self.h("dve"), self.h("act")
            nt = NHB + NJ
            act0 = c["s_act"]
            self.wait("sy", "s_dve", c["s_dve"])
            for it in range(nt):
                slot = it % 2
                wst = sb["wst0"] if slot == 0 else sb["wst1"]
                csem = "c_wf0" if slot == 0 else "c_wf1"
                wid = 2048 if it < NHB else 2816
                m = 0 if it < NHB else 1
                if it >= 2:
                    self.wait("sy", "s_act", act0 + (it - 2) * 2 + 2)
                self.dma("sy", csem, wst[:, 0:wid],
                         self.ap["wFb"][it, :, :] if it < NHB
                         else self.ap["w3b"][it - NHB, :, :])
                self.wait("act", csem, c[csem])
                self.wait("act", "c_trw", 16 * it)
                if "tern" in self.taps and it >= 1:
                    self.wait("act", "c_tap", 16)
                i = act.activation(sb["qu"][:, 0:wid], wst[:, 0:wid], AF.Abs)
                self.fence("act", i)
                act.activation(sb["qt"][:, 0:wid], sb["qu"][:, 0:wid], AF.Sign,
                               scale=sb["invg"][:, m:m + 1],
                               bias=sb["neghalf"][:, 0:1])
                i = act.activation(sb["qs"][:, 0:wid], wst[:, 0:wid], AF.Sign)
                self.inc(i, "s_act")
                self.wait("dve", "s_act", c["s_act"])
                i = dve.scalar_tensor_tensor(sb["qo"][:, 0:wid],
                                             sb["qt"][:, 0:wid], 0.0,
                                             sb["qs"][:, 0:wid],
                                             ALU.max, ALU.mult)
                self.inc(i, "s_dve")
                self.wait("gp", "s_dve", c["s_dve"])
                if it < NHB:
                    self.gp_sw_dma("c_trw",
                                   lambda w, it=it: self.ap["q12_sh"][w, it, :, :],
                                   lambda w: sb["qo"][:, 0:2048])
                else:
                    j = it - NHB
                    self.gp_sw_dma(
                        "c_trw",
                        lambda w, j=j: self.ap["w3_sh"][j, :,
                                                        22 * w:22 * w + 22, :],
                        lambda w: sb["qo"][:, 0:2816].rearrange(
                            "p (a b) -> p a b", a=22))

            self.wait("gp", "c_trw", c["c_trw"])
            if "tern" in self.taps:
                self.gp_sw_dma("c_tap",
                               lambda w: self.ap["tap_tern"][:, :],
                               lambda w: self.ap["q12_sh"][w, 0, :, :])
            self.set_flag(0)
        self.wswitch(body)

    def ln_subtile(self, xsrc, xn_dst):
        sb, c = self.sb, self.c
        act, dve = self.h("act"), self.h("dve")
        st = sb["stats"]
        i = act.activation(sb["lnscr"][:, 0:2048], xsrc, AF.Square,
                           accum_out=st[:, 1:2])
        self.inc(i, "s_act")
        i = dve.reduce_sum(st[:, 0:1], xsrc, axis=AX.X)
        self.fence("dve", i)
        self.wait("dve", "s_act", c["s_act"])
        i = dve.tensor_scalar(st[:, 2:3], st[:, 0:1], 1.0 / D, 0.0,
                              ALU.mult, ALU.add)
        dve.tensor_scalar(st[:, 3:4], st[:, 1:2], 1.0 / D, 0.0,
                          ALU.mult, ALU.add)
        self.fence("dve", i)
        i = dve.tensor_scalar(st[:, 6:7], st[:, 2:3], st[:, 2:3], None,
                              ALU.mult)
        self.fence("dve", i)
        i = dve.tensor_sub(st[:, 3:4], st[:, 3:4], st[:, 6:7])
        self.inc(i, "s_dve")
        self.wait("act", "s_dve", c["s_dve"])
        i = act.activation(st[:, 4:5], st[:, 3:4], AF.Ln, bias=st[:, 7:8])
        self.fence("act", i)
        i = act.activation(st[:, 5:6], st[:, 4:5], AF.Exp, scale=-0.5)
        self.inc(i, "s_act")
        self.wait("dve", "s_act", c["s_act"])
        i = dve.tensor_scalar(xn_dst, xsrc, st[:, 2:3], st[:, 5:6],
                              ALU.subtract, ALU.mult)
        self.inc(i, "s_dve")

    def rope6(self, dst, src_ps, cs, sn, wid=512):
        dve = self.h("dve")
        sb = self.sb
        t1 = sb["rtmp1"][:, 0:wid]
        t2 = sb["rtmp2"][:, 0:wid]
        dve.tensor_mul(t1, src_ps[0:32, :], cs)
        i = dve.tensor_mul(t2, src_ps[64:96, :], sn)
        self.fence("dve", i)
        i = dve.tensor_add(dst[0:32, :], t1, t2)
        self.fence("dve", i)
        dve.tensor_mul(t1, src_ps[64:96, :], cs)
        i = dve.tensor_mul(t2, src_ps[0:32, :], sn)
        self.fence("dve", i)
        i = dve.tensor_sub(dst[64:96, :], t1, t2)
        self.inc(i, "s_dve")

    def p4_a1(self):
        self.wswitch(self._p4_body)

    def _p4_body(self, w):
        sb, c, ps = self.sb, self.c, self.ps
        pe, act = self.h("pe"), self.h("act")
        xslots = (sb["x0"], sb["x1"])
        xnslots = (sb["xnT0"], sb["xnT1"])
        xfslots = (sb["xnF0"], sb["xnF1"])
        kslots = (sb["kst0"], sb["kst1"])
        vslots = (sb["vst0"], sb["vst1"])

        norm_done = []
        tr_done = []
        evac_done = []
        krope = []
        kvw_k = []
        vev = []
        kvw_v = []
        xf_free = []

        self.wait("pe", "c_const", c["c_const"])
        self.wait("pe", "c_wkv", c["c_wkv"])
        self.wait("act", "c_const", c["c_const"])
        self.wait("dve", "c_trw", c["c_trw"])
        self.wait("sy", "s_act", c["s_act"])
        self.wait("sy", "s_dve", c["s_dve"])

        for t in range(NTIL):
            for s in range(4):
                u = 4 * t + s
                slot = u % 2
                csem = "c_x0" if slot == 0 else "c_x1"
                if u >= 2:
                    self.wait("sy", "s_dve", norm_done[u - 2])
                self.dma("sy", csem, xslots[slot][:],
                         self.ap["x_all"][512 * t + 128 * s:
                                          512 * t + 128 * s + 128, :])
                self.wait("act", csem, c[csem])
                self.wait("dve", csem, c[csem])
                if u >= 2:
                    self.wait("dve", "s_pe", tr_done[u - 2])
                self.ln_subtile(xslots[slot][:], xnslots[slot][:])
                norm_done.append(c["s_dve"])
                self.wait("pe", "s_dve", c["s_dve"])
                for half in range(2):
                    if len(evac_done) >= 2:
                        self.wait("pe", "s_act", evac_done[-2])
                    for jj in range(8):
                        j = half * 8 + jj
                        i = pe.transpose(
                            ps["T"][:, 1024 * half + 128 * jj:
                                    1024 * half + 128 * jj + 128],
                            xnslots[slot][:, 128 * j:128 * j + 128],
                            sb["identbf"][:])
                    self.inc(i, "s_pe")
                    self.wait("act", "s_pe", c["s_pe"])
                    if s == 0 and half == 0 and t >= 2:
                        self.wait("act", "s_pe", xf_free[t - 2])
                    xf3 = xfslots[t % 2][:].rearrange("p (j r) -> p j r", j=16)
                    i = act.activation(
                        xf3[:, half * 8:half * 8 + 8, 128 * s:128 * s + 128],
                        ps["T"][:, 1024 * half:1024 * half + 1024].rearrange(
                            "p (j r) -> p j r", j=8),
                        AF.Copy)
                    self.inc(i, "s_act")
                    evac_done.append(c["s_act"])
                tr_done.append(c["s_pe"])
            xf = xfslots[t % 2]
            self.wait("pe", "s_act", evac_done[-1])
            for hl in range(8):
                aslot = hl % 4
                if len(krope) >= 4:
                    self.wait("pe", "s_act", krope[-4][0])
                    self.wait("pe", "s_dve", krope[-4][1])
                for j in range(NJ):
                    i = pe.matmul(ps["A"][:, 512 * aslot:512 * aslot + 512],
                                  sb["wkv"][:, 2048 * j + 128 * hl:
                                            2048 * j + 128 * hl + 128],
                                  xf[:, 512 * j:512 * j + 512],
                                  start=(j == 0), stop=(j == NJ - 1))
                self.inc(i, "s_pe")
                kslot = hl % 2
                self.wait("act", "s_pe", c["s_pe"])
                if len(kvw_k) >= 2:
                    self.wait("act", "c_kvw", kvw_k[-2])
                i = act.activation(kslots[kslot][:],
                                   ps["A"][:, 512 * aslot:512 * aslot + 512],
                                   AF.Copy)
                self.inc(i, "s_act")
                a_cp = c["s_act"]
                self.wait("dve", "s_act", c["s_act"])
                q4 = t // 8
                cs = sb["cosf"][32 * q4:32 * q4 + 32,
                                (512 * t) % 4096:(512 * t) % 4096 + 512]
                sn = sb["sinf"][32 * q4:32 * q4 + 32,
                                (512 * t) % 4096:(512 * t) % 4096 + 512]
                self.rope6(kslots[kslot],
                           ps["A"][:, 512 * aslot:512 * aslot + 512], cs, sn)
                krope.append((a_cp, c["s_dve"]))
                self.wait("gp", "s_dve", c["s_dve"])
                self.gp_sw_dma(
                    "c_kvw",
                    lambda w, hl=hl, t=t: self.ap["kT_sh"][8 * w + hl, :,
                                                           512 * t:512 * t + 512],
                    lambda w, kslot=kslot: kslots[kslot][:])
                kvw_k.append(c["c_kvw"])
            for g in range(2):
                for s2 in range(4):
                    bslot = (4 * g + s2) % 2
                    if len(vev) >= 2:
                        self.wait("pe", "s_act", vev[-2])
                    for j in range(NJ):
                        i = pe.matmul(
                            ps["B"][:, 512 * bslot:512 * bslot + 512],
                            xf[:, 512 * j + 128 * s2:512 * j + 128 * s2 + 128],
                            sb["wkv"][:, 2048 * j + 1024 + 512 * g:
                                      2048 * j + 1024 + 512 * g + 512],
                            start=(j == 0), stop=(j == NJ - 1))
                    self.inc(i, "s_pe")
                    self.wait("act", "s_pe", c["s_pe"])
                    if len(kvw_v) >= 2:
                        self.wait("act", "c_kvw", kvw_v[-2])
                    i = act.activation(vslots[bslot][:],
                                       ps["B"][:, 512 * bslot:512 * bslot + 512],
                                       AF.Copy)
                    self.inc(i, "s_act")
                    vev.append(c["s_act"])
                    self.wait("gp", "s_act", c["s_act"])
                    for hh in range(4):
                        self.gp_sw_dma(
                            "c_kvw",
                            lambda w, g=g, hh=hh, t=t, s2=s2:
                                self.ap["V_sh"][8 * w + 4 * g + hh, t, :, s2, :],
                            lambda w, bslot=bslot, hh=hh:
                                vslots[bslot][:, 128 * hh:128 * hh + 128])
                    kvw_v.append(c["c_kvw"])
            xf_free.append(c["s_pe"])
        self.wait("gp", "c_kvw", c["c_kvw"])
        self.set_flag(1)
        if "xnF" in self.taps:
            self.wait("gp", "s_act", c["s_act"])
            self.dma("gp", "c_tap", self.ap["tap_xnF"][:, :],
                     xfslots[(NTIL - 1) % 2][:])
        if "kT" in self.taps:
            self.gp_sw_dma("c_tap", lambda w: self.ap["tap_kT"][:, :, :],
                           lambda w: self.ap["kT_sh"][8 * w:8 * w + 8, :, :])
        if "V" in self.taps:
            self.gp_sw_dma("c_tap", lambda w: self.ap["tap_V"][:, :, :, :, :],
                           lambda w: self.ap["V_sh"][8 * w:8 * w + 8, :, :, :, :])

    def p5_own(self):
        sb, c, ps = self.sb, self.c, self.ps
        pe, act, dve = self.h("pe"), self.h("act"), self.h("dve")
        self.wait("sy", "s_pe", c["s_pe"])
        self.dma("sy", "c_const", sb["coso"][:], self.ap["cos_own"][:, :])
        self.dma("sy", "c_const", sb["sino"][:], self.ap["sin_own"][:, :])
        xoslots = (sb["xo0"], sb["xo1"])
        xnslots = (sb["xnT0"], sb["xnT1"])
        norm5, tr5, evac5, xts_ev, xtw_hist = [], [], [], [], []
        for so in range(8):
            slot = so % 2
            csem = "c_xo0" if slot == 0 else "c_xo1"
            if so >= 2:
                self.wait("sy", "s_dve", norm5[so - 2])
                self.wait("sy", "s_pe", tr5[so - 2])
            self.dma("sy", csem, xoslots[slot][:],
                     self.ap["x_own"][128 * so:128 * so + 128, :])
            self.wait("act", csem, c[csem])
            self.wait("dve", csem, c[csem])
            if so >= 2:
                self.wait("dve", "s_pe", tr5[so - 2])
            self.ln_subtile(xoslots[slot][:], xnslots[slot][:])
            norm5.append(c["s_dve"])
            self.wait("pe", "s_dve", c["s_dve"])
            for half in range(2):
                if len(evac5) >= 2:
                    self.wait("pe", "s_act", evac5[-2])
                for jj in range(8):
                    j = half * 8 + jj
                    i = pe.transpose(ps["T"][:, 1024 * half + 128 * jj:
                                             1024 * half + 128 * jj + 128],
                                     xnslots[slot][:, 128 * j:128 * j + 128],
                                     sb["identbf"][:])
                self.inc(i, "s_pe")
                self.wait("act", "s_pe", c["s_pe"])
                xo3 = sb["xnFo"][:].rearrange("p (j r) -> p j r", j=16)
                i = act.activation(
                    xo3[:, half * 8:half * 8 + 8, 128 * so:128 * so + 128],
                    ps["T"][:, 1024 * half:1024 * half + 1024].rearrange(
                        "p (j r) -> p j r", j=8),
                    AF.Copy)
                self.inc(i, "s_act")
                evac5.append(c["s_act"])
            for grp in range(4):
                gidx = 4 * so + grp
                if gidx >= 4:
                    self.wait("pe", "s_dve", xts_ev[gidx - 4])
                for jj in range(4):
                    i = pe.transpose(ps["A"][:, 512 * grp + 128 * jj:
                                             512 * grp + 128 * jj + 128],
                                     xoslots[slot][:, 512 * grp + 128 * jj:
                                                    512 * grp + 128 * jj + 128],
                                     sb["ident32"][:])
                self.inc(i, "s_pe")
                self.wait("dve", "s_pe", c["s_pe"])
                if so >= 1:
                    self.wait("dve", "c_xtw", xtw_hist[so - 1])
                i = dve.tensor_copy(sb["xts"][:, 512 * grp:512 * grp + 512],
                                    ps["A"][:, 512 * grp:512 * grp + 512])
                self.inc(i, "s_dve")
                xts_ev.append(c["s_dve"])
            tr5.append(c["s_pe"])
            self.wait("sy", "s_dve", c["s_dve"])
            self.dma("sy", "c_xtw",
                     self.ap["xT_own"][:, :, 128 * so:128 * so + 128],
                     sb["xts"][:].rearrange("p (j r) -> p j r", j=16))
            xtw_hist.append(c["c_xtw"])
        qmm, qrope = [], []
        self.wait("pe", "s_act", evac5[-1])
        self.wait("pe", "s_dve", xts_ev[-1])
        self.wait("dve", "c_const", c["c_const"])
        for hh in range(H):
            slot = hh % 2
            csem = "c_wq0" if slot == 0 else "c_wq1"
            if hh >= 2:
                self.wait("gp", "s_pe", qmm[hh - 2])
            wq = sb["wqt0"] if slot == 0 else sb["wqt1"]
            self.dma("gp", csem, wq[:], self.ap["wqTb"][hh, :, :])
            pq = ps["B"][:, 0:1024] if hh % 2 == 0 else ps["A"][:, 0:1024]
            self.wait("pe", csem, c[csem])
            if hh >= 2:
                self.wait("pe", "s_dve", qrope[hh - 2])
            for j in range(NJ):
                for half in range(2):
                    i = pe.matmul(pq[:, 512 * half:512 * half + 512],
                                  wq[:, 128 * j:128 * j + 128],
                                  sb["xnFo"][:, 1024 * j + 512 * half:
                                             1024 * j + 512 * half + 512],
                                  start=(j == 0), stop=(j == NJ - 1))
            self.inc(i, "s_pe")
            qmm.append(c["s_pe"])
            self.wait("act", "s_pe", c["s_pe"])
            i = act.activation(sb["qT"][:, 1024 * hh:1024 * hh + 1024], pq,
                               AF.Copy)
            self.inc(i, "s_act")
            self.wait("dve", "s_act", c["s_act"])
            for half in range(2):
                self.rope6(sb["qT"][:, 1024 * hh + 512 * half:
                                    1024 * hh + 512 * half + 512],
                           pq[:, 512 * half:512 * half + 512],
                           sb["coso"][:, 512 * half:512 * half + 512],
                           sb["sino"][:, 512 * half:512 * half + 512])
            qrope.append(c["s_dve"])
        if "qT" in self.taps:
            self.wait("gp", "s_dve", c["s_dve"])
            self.dma("gp", "c_tap", self.ap["tap_qT"][:, :], sb["qT"][:])
        if "xT" in self.taps:
            self.wait("gp", "c_xtw", c["c_xtw"])
            self.dma("gp", "c_tap", self.ap["tap_xT"][:, :, :],
                     self.ap["xT_own"][:, :, :])

    def p6_ffn(self):
        sb, c, ps = self.sb, self.c, self.ps
        pe, act, dve = self.h("pe"), self.h("act"), self.h("dve")
        self.poll(0)
        self.wait("sy", "s_gp", c["s_gp"])
        self.wait("sy", "s_pe", c["s_pe"])
        self.wait("sy", "s_dve", c["s_dve"])
        q1s = (sb["q120"], sb["q121"])
        q2s = (sb["q122"], sb["q123"])
        x1mm, x2mm, sgd, hmul = [], [], [], []
        for hb in range(NHB):
            slot = hb % 2
            s1 = "c_q0" if slot == 0 else "c_q1"
            s2 = "c_q2" if slot == 0 else "c_q3"
            if hb >= 2:
                self.wait("sy", "s_pe", x2mm[hb - 2])
            self.dma("sy", s1, q1s[slot][:], self.ap["q12_sh"][0, hb, :, :])
            self.dma("sy", s2, q2s[slot][:], self.ap["q12_sh"][1, hb, :, :])
            self.wait("pe", s1, c[s1])
            if hb >= 1:
                self.wait("pe", "s_act", sgd[hb - 1])
            for j in range(NJ):
                for half in range(2):
                    i = pe.matmul(ps["B"][:, 512 * half:512 * half + 512],
                                  q1s[slot][:, 128 * j:128 * j + 128],
                                  sb["xnFo"][:, 1024 * j + 512 * half:
                                             1024 * j + 512 * half + 512],
                                  start=(j == 0), stop=(j == NJ - 1))
            self.inc(i, "s_pe")
            x1mm.append(c["s_pe"])
            pa = ps["A"][:, 1024 * (hb % 2):1024 * (hb % 2) + 1024]
            self.wait("pe", s2, c[s2])
            if hb >= 2:
                self.wait("pe", "s_dve", hmul[hb - 2])
            for j in range(NJ):
                for half in range(2):
                    i = pe.matmul(pa[:, 512 * half:512 * half + 512],
                                  q2s[slot][:, 128 * j:128 * j + 128],
                                  sb["xnFo"][:, 1024 * j + 512 * half:
                                             1024 * j + 512 * half + 512],
                                  start=(j == 0), stop=(j == NJ - 1))
            self.inc(i, "s_pe")
            x2mm.append(c["s_pe"])
            self.wait("act", "s_pe", x1mm[hb])
            if hb >= 1:
                self.wait("act", "s_dve", hmul[hb - 1])
            i = act.activation(sb["sg"][:], ps["B"][:, 0:1024], AF.Sigmoid)
            self.inc(i, "s_act")
            sgd.append(c["s_act"])
            self.wait("dve", "s_act", c["s_act"])
            self.wait("dve", "s_pe", x2mm[hb])
            i = dve.tensor_mul(sb["hT"][:, 1024 * hb:1024 * hb + 1024],
                               sb["sg"][:], pa)
            self.inc(i, "s_dve")
            hmul.append(c["s_dve"])
        if "hT" in self.taps:
            self.wait("gp", "s_dve", c["s_dve"])
            self.dma("gp", "c_tap", self.ap["tap_hT"][:, :], sb["hT"][:])

    def p7_attn(self):
        sb, c = self.sb, self.c
        self.poll(1)
        self.wait("sy", "s_gp", c["s_gp"])
        self.wait("sy", "s_dve", c["s_dve"])
        if self.me in SWITCHED_P7:
            it = self.real.Switch(self.rank_sv, NCORE)
        else:
            it = iter((0,))
        entry = dict(c)
        end_state = None
        for r in it:
            self.c = dict(entry)
            self.emit_p7_body(r)
            if end_state is None:
                end_state = dict(self.c)
            else:
                assert end_state == self.c, f"P7 count mismatch rank {r}"
        self.c = end_state
        c = self.c
        if "yhat" in self.taps:
            self.wait("gp", "s_dve", c["s_dve"])
            self.dma("gp", "c_tap", self.ap["tap_yhat"][0, :, :], sb["yh0"][:])
            self.dma("gp", "c_tap", self.ap["tap_yhat"][1, :, :], sb["yh1"][:])

    def emit_p7_body(self, r):
        sb, c, ps = self.sb, self.c, self.ps
        pe, act, dve = self.h("pe"), self.h("act"), self.h("dve")
        kslots = (sb["k0"], sb["k1"])
        vslots = (sb["v0"], sb["v1"])
        ksem = ("c_k0", "c_k1")
        vsem = ("c_v0", "c_v1")
        pts = (sb["pt0"], sb["pt1"], sb["pt2"], sb["pt3"])
        chunks = (r, 15 - r)

        last_k_cons = [0, 0]
        last_v_cons = [0, 0]
        last_exp = [0, 0, 0, 0]
        last_pt_cons = [0, 0, 0, 0]
        last_pt_dve = [0, 0, 0, 0]
        last_ptm_cons = [0, 0]
        mask_of = {}
        evac_prev = 0
        gi = 0

        for qci in range(2):
            m = chunks[qci]
            for hh in range(H):
                n = 4 * (m + 1)
                kl = {}
                vl = {}

                def load_kg(kg):
                    slot = kg % 2
                    self.wait("sy", "s_pe", last_k_cons[slot])
                    self.dma("sy", ksem[slot], kslots[slot][:],
                             self.ap["kT_sh"][hh, :, 512 * kg:512 * kg + 512])
                    kl[kg] = c[ksem[slot]]
                    self.wait("sy", "s_pe", last_v_cons[slot])
                    self.dma("sy", vsem[slot], vslots[slot][:],
                             self.ap["V_sh"][hh, kg, :, :, :].rearrange(
                                 "p a b -> p (a b)"))
                    vl[kg] = c[vsem[slot]]

                load_kg(0)
                if m >= 1:
                    load_kg(1)
                exp_of = {}

                def emit_S(i):
                    gidx = gi + i
                    slot = gidx % 4
                    kg = i // 4
                    b = i % 4
                    self.wait("pe", "s_act", last_exp[slot])
                    self.wait("pe", ksem[kg % 2], kl[kg])
                    i2 = pe.matmul(ps["A"][:, 512 * slot:512 * slot + 512],
                                   kslots[kg % 2][:, 128 * b:128 * b + 128],
                                   sb["qT"][:, 1024 * hh + 512 * qci:
                                            1024 * hh + 512 * qci + 512],
                                   start=True, stop=True)
                    self.inc(i2, "s_pe")
                    if b == 3:
                        last_k_cons[kg % 2] = c["s_pe"]
                    self.wait("act", "s_pe", c["s_pe"])
                    self.wait("act", "s_pe", last_pt_cons[slot])
                    self.wait("act", "s_dve", last_pt_dve[slot])
                    i2 = act.activation(pts[slot][:],
                                        ps["A"][:, 512 * slot:512 * slot + 512],
                                        AF.Exp, scale=SCALE)
                    self.inc(i2, "s_act")
                    last_exp[slot] = c["s_act"]
                    exp_of[gidx] = c["s_act"]
                    if i >= n - 4:
                        dslot = (i - (n - 4)) % 2
                        self.wait("dve", "s_act", c["s_act"])
                        self.wait("dve", "s_pe", last_ptm_cons[dslot])
                        i2 = dve.tensor_mul(
                            (sb["ptm0"] if dslot == 0 else sb["ptm1"])[:],
                            pts[slot][:],
                            sb["masks"][:, 512 * b:512 * b + 512])
                        self.inc(i2, "s_dve")
                        mask_of[gidx] = (c["s_dve"], dslot)
                        last_pt_dve[slot] = c["s_dve"]

                def emit_AV(i):
                    gidx = gi + i
                    slot = gidx % 4
                    kg = i // 4
                    b = i % 4
                    self.wait("pe", "s_act", exp_of[gidx])
                    src_pt = pts[slot][:]
                    if gidx in mask_of:
                        mcount, dslot = mask_of[gidx]
                        self.wait("pe", "s_dve", mcount)
                        src_pt = (sb["ptm0"] if dslot == 0 else sb["ptm1"])[:]
                    self.wait("pe", vsem[kg % 2], vl[kg])
                    if i == 0 and evac_prev:
                        self.wait("pe", "s_dve", evac_prev)
                    pe.matmul(ps["B"][:, 0:512],
                              vslots[kg % 2][:, 128 * b:128 * b + 128],
                              src_pt,
                              start=(i == 0), stop=(i == n - 1))
                    i2 = pe.matmul(ps["B"][:, 512:1024], sb["onesbf"][:],
                                   src_pt,
                                   start=(i == 0), stop=(i == n - 1))
                    self.inc(i2, "s_pe")
                    if gidx in mask_of:
                        last_ptm_cons[mask_of[gidx][1]] = c["s_pe"]
                    else:
                        last_pt_cons[slot] = c["s_pe"]
                    if b == 3:
                        last_v_cons[kg % 2] = c["s_pe"]

                for i in range(n + 1):
                    if i < n:
                        emit_S(i)
                    if i >= 1:
                        emit_AV(i - 1)
                        if (i - 1) % 4 == 3 and (i - 1) // 4 + 2 <= m:
                            load_kg((i - 1) // 4 + 2)
                gi += n
                self.wait("dve", "s_pe", c["s_pe"])
                dve.reciprocal(sb["recip"][:], ps["B"][:, 512:1024])
                yh = sb["yh0"] if qci == 0 else sb["yh1"]
                i2 = dve.scalar_tensor_tensor(
                    yh[:, 512 * hh:512 * hh + 512], ps["B"][:, 0:512],
                    1.0 / 3.0, sb["recip"][:], ALU.mult, ALU.mult)
                self.inc(i2, "s_dve")
                evac_prev = c["s_dve"]

    def p8_out(self):
        sb, c, ps = self.sb, self.c, self.ps
        pe, dve = self.h("pe"), self.h("dve")
        w3s = (sb["w3j0"], sb["w3j1"])
        wos = (sb["woj0"], sb["woj1"])
        xts = (sb["xt0"], sb["xt1"])
        osts = (sb["ost0"], sb["ost1"])
        w3sem = ("c_w30", "c_w31")
        wosem = ("c_wo0", "c_wo1")
        xtsem = ("c_xt0", "c_xt1")
        mmend, evac8, outw = [], [], []
        if "w3sh" in self.taps:
            self.dma("gp", "c_tap", self.ap["tap_w3sh"][:, :],
                     self.ap["w3_sh"][0, :, :, :].rearrange("p a b -> p (a b)"))
        if "q12a" in self.taps:
            self.dma("gp", "c_tap", self.ap["tap_q12a"][:, :],
                     self.ap["q12_sh"][1, 0, :, :])
        self.wait("sy", "s_pe", c["s_pe"])
        self.wait("gp", "s_pe", c["s_pe"])
        self.wait("sy", "c_xtw", c["c_xtw"])
        for j in range(NJ):
            slot = j % 2
            if j >= 2:
                self.wait("sy", "s_pe", mmend[2 * (j - 2) + 1])
                self.wait("gp", "s_pe", mmend[2 * (j - 2) + 1])
            self.dma("sy", w3sem[slot], w3s[slot][:],
                     self.ap["w3_sh"][j, :, :, :].rearrange("p a b -> p (a b)"))
            self.dma("gp", wosem[slot], wos[slot][:], self.ap["woTb"][j, :, :])
            for qci in range(2):
                u = 2 * j + qci
                xslot = u % 2
                if u >= 2:
                    self.wait("sy", "s_dve", evac8[u - 2])
                self.dma("sy", xtsem[xslot], xts[xslot][:],
                         self.ap["xT_own"][:, j, 512 * qci:512 * qci + 512])
                pslot = u % 4
                pa = ps["A"][:, 512 * pslot:512 * pslot + 512]
                self.wait("pe", w3sem[slot], c[w3sem[slot]])
                self.wait("pe", wosem[slot], c[wosem[slot]])
                if u >= 4:
                    self.wait("pe", "s_dve", evac8[u - 4])
                yh = sb["yh0"] if qci == 0 else sb["yh1"]
                for hc in range(16):
                    pe.matmul(pa, wos[slot][:, 128 * hc:128 * hc + 128],
                              yh[:, 512 * hc:512 * hc + 512],
                              start=(hc == 0), stop=False)
                for hb in range(NHB):
                    i = pe.matmul(pa, w3s[slot][:, 128 * hb:128 * hb + 128],
                                  sb["hT"][:, 1024 * hb + 512 * qci:
                                           1024 * hb + 512 * qci + 512],
                                  start=False, stop=(hb == NHB - 1))
                self.inc(i, "s_pe")
                mmend.append(c["s_pe"])
                self.wait("dve", "s_pe", c["s_pe"])
                self.wait("dve", xtsem[xslot], c[xtsem[xslot]])
                if u >= 2:
                    self.wait("dve", "c_out", outw[u - 2])
                i = dve.tensor_add(osts[xslot][:], pa, xts[xslot][:])
                self.inc(i, "s_dve")
                evac8.append(c["s_dve"])
                self.wait("gp", "s_dve", c["s_dve"])
                self.dma("gp", "c_out",
                         self.ap["outT"][128 * j:128 * j + 128,
                                         512 * qci:512 * qci + 512],
                         osts[xslot][:])
                outw.append(c["c_out"])

    def p9_final(self):
        self.wait("gp", "c_out", self.c["c_out"])
        self.wait("gp", "c_fl", self.c["c_fl"])
        if self.taps:
            self.wait("gp", "c_tap", self.c["c_tap"])

    def run(self):
        self.setup_regs()
        self.p0_consts()
        self.p1_wkv()
        if self.upto >= 2:
            self.p2_gamma()
        if self.upto >= 3:
            self.p3_quantize()
        if self.upto >= 4:
            self.p4_a1()
        if self.upto >= 5:
            self.p5_own()
        if self.upto >= 6:
            self.p6_ffn()
        if self.upto >= 7:
            self.p7_attn()
        if self.upto >= 8:
            self.p8_out()
        self.p9_final()
        return self.c


def build_nc(upto=None, taps=None):
    upto = UPTO if upto is None else upto
    taps = tuple(DEBUG_TAPS if taps is None else taps)
    nc = bass.Bass("TRN2", target_bir_lowering=False, num_devices=NCORE,
                   detect_race_conditions=False)

    ap = {}

    def din(name, shape, dtype=F32):
        ap[name] = nc.dram_tensor(name, list(shape), dtype,
                                  kind="ExternalInput").ap()

    din("x_all", (T, D))
    din("x_own", (1024, D))
    din("wkvTb", (NJ, 128, 2048))
    din("wqTb", (16, 128, 2048))
    din("woTb", (16, 128, 2048))
    din("wFb", (NHB, 128, 2048))
    din("w3b", (NJ, 128, 22 * 128))
    din("cosb", (64, 4096))
    din("sinb", (64, 4096))
    din("cos_own", (32, 1024))
    din("sin_own", (32, 1024))
    din("ident32", (128, 128))
    din("masks", (128, 2048))
    din("nonce", (1, 8), I32)

    ap["outT"] = nc.dram_tensor("outT", [D, 1024], F32,
                                kind="ExternalOutput").ap()

    for name, shape in [("w3sh", (128, NHB * 128)), ("q12a", (128, 2048)),
                        ("xnF", (128, 16 * 512)), ("qT", (128, 16 * 1024)),
                        ("hT", (128, 44 * 1024)), ("gam", (1, 8)),
                        ("tern", (128, 2048)), ("yhat", (2, 128, 16 * 512)),
                        ("xT", (128, NJ, 1024)), ("kT", (8, 128, T)),
                        ("V", (8, NTIL, 128, 4, 128))]:
        if name in taps:
            ap["tap_" + name] = nc.dram_tensor("tap_" + name, list(shape),
                                               F32, kind="ExternalOutput").ap()

    ap["xT_own"] = nc.dram_tensor("xT_own", [128, NJ, 1024], F32).ap()
    ap["kT_sh"] = nc.dram_tensor("kT_sh", [16, 128, T], BF16,
                                 addr_space="Shared").ap()
    ap["V_sh"] = nc.dram_tensor("V_sh", [16, NTIL, 128, 4, 128], BF16,
                                addr_space="Shared").ap()
    ap["q12_sh"] = nc.dram_tensor("q12_sh", [2, NHB, 128, 2048], BF16,
                                  addr_space="Shared").ap()
    ap["w3_sh"] = nc.dram_tensor("w3_sh", [NJ, 128, NHB, 128], BF16,
                                 addr_space="Shared").ap()
    ap["absum_sh"] = nc.dram_tensor("absum_sh", [2, 8], F32,
                                    addr_space="Shared").ap()
    ap["flags_sh"] = nc.dram_tensor("flags_sh", [2, 4, 8], I32,
                                    addr_space="Shared").ap()

    from contextlib import ExitStack
    with ExitStack() as ctx:
        E = ctx.enter_context
        sb = {}

        def S(name, seg, shape, dtype, extra=0):
            off = SB_BASE + OFF[seg] + extra
            nbytes = int(np.prod(shape[1:])) * mybir.dt.size(dtype)
            assert off + nbytes <= SB_BASE + SB_LIMIT, (name, off, nbytes)
            sb[name] = nc.alloc_sbuf_tensor_at("sb_" + name, list(shape),
                                               dtype, offset=off)

        S("ident32", "consts", (128, 128), F32, 0)
        S("identbf", "consts", (128, 128), BF16, 512)
        S("onesbf", "consts", (128, 128), BF16, 768)
        S("masks", "consts", (128, 2048), BF16, 1024)
        S("noncesb", "consts", (1, 8), I32, 5184)
        S("flagsb", "consts", (1, 8), I32, 5248)
        S("absb", "consts", (1, 8), F32, 5312)
        S("onesf1", "consts", (1, 128), F32, 5376)
        S("gamv", "consts", (1, 8), F32, 5984)
        S("invg", "consts", (128, 2), F32, 6016)
        S("onescol", "consts", (128, 1), F32, 6048)
        S("neghalf", "consts", (128, 1), F32, 6080)
        S("recip", "consts", (128, 512), F32, 6144)
        S("cosf", "cosf", (64, 4096), F32)
        S("sinf", "sinf", (64, 4096), F32)
        S("wkv", "wkv", (128, NJ * 2048), BF16)
        S("wst0", "wstage", (128, 2816), F32, 0)
        S("wst1", "wstage", (128, 2816), F32, 11264)
        S("qu", "qscr", (128, 2816), F32, 0)
        S("qo", "qscr", (128, 2816), BF16, 0)
        S("qs", "qscr", (128, 2816), BF16, 11264)
        S("qt", "qscr", (128, 2816), BF16, 16896)
        S("lnscr", "lnscr", (128, 2048), F32)
        S("stats", "stats", (128, 8), F32)
        S("x0", "x", (128, 2048), F32, 0)
        S("x1", "x", (128, 2048), F32, 8192)
        S("xnT0", "xnT", (128, 2048), BF16, 0)
        S("xnT1", "xnT", (128, 2048), BF16, 4096)
        S("xnF0", "xnFc", (128, 16 * 512), BF16, 0)
        S("xnF1", "xnFc", (128, 16 * 512), BF16, 16384)
        S("kst0", "kvst", (128, 512), BF16, 0)
        S("kst1", "kvst", (128, 512), BF16, 1024)
        S("vst0", "kvst", (128, 512), BF16, 2048)
        S("vst1", "kvst", (128, 512), BF16, 3072)
        S("rtmp1", "kvst", (32, 512), F32, 4096)
        S("rtmp2", "kvst", (32, 512), F32, 6144)
        S("xnFo", "xnFo", (128, 16 * 1024), BF16)
        S("qT", "qT", (128, 16 * 1024), BF16)
        S("wqt0", "wqt", (128, 2048), BF16, 0)
        S("wqt1", "wqt", (128, 2048), BF16, 4096)
        S("xo0", "xo", (128, 2048), F32, 0)
        S("xo1", "xo", (128, 2048), F32, 8192)
        S("xts", "xts", (128, 2048), F32)
        S("coso", "trig", (32, 1024), F32, 0)
        S("sino", "trig", (32, 1024), F32, 4096)
        S("hT", "hT", (128, NHB * 1024), BF16)
        S("q120", "q12", (128, 2048), BF16, 0)
        S("q121", "q12", (128, 2048), BF16, 4096)
        S("q122", "q12", (128, 2048), BF16, 8192)
        S("q123", "q12", (128, 2048), BF16, 12288)
        S("sg", "sg", (128, 1024), BF16)
        S("k0", "att", (128, 512), BF16, 0)
        S("k1", "att", (128, 512), BF16, 1024)
        S("v0", "att", (128, 512), BF16, 2048)
        S("v1", "att", (128, 512), BF16, 3072)
        S("pt0", "att", (128, 512), BF16, 4096)
        S("pt1", "att", (128, 512), BF16, 5120)
        S("pt2", "att", (128, 512), BF16, 6144)
        S("pt3", "att", (128, 512), BF16, 7168)
        S("ptm0", "att", (128, 512), BF16, 8192)
        S("ptm1", "att", (128, 512), BF16, 9216)
        S("yh0", "yhat", (128, 16 * 512), BF16, 0)
        S("yh1", "yhat", (128, 16 * 512), BF16, 16384)
        S("w3j0", "out", (128, NHB * 128), BF16, 0)
        S("w3j1", "out", (128, NHB * 128), BF16, 11264)
        S("woj0", "out", (128, 2048), BF16, 22528)
        S("woj1", "out", (128, 2048), BF16, 26624)
        S("xt0", "out", (128, 512), F32, 30720)
        S("xt1", "out", (128, 512), F32, 32768)
        S("ost0", "out", (128, 512), F32, 34816)
        S("ost1", "out", (128, 512), F32, 36864)

        ps = {"A": E(nc.psum_tensor("psA", [128, 2048], F32)),
              "B": E(nc.psum_tensor("psB", [128, 1024], F32)),
              "T": E(nc.psum_tensor("psT", [128, 2048], BF16))}

        sems = {}
        for sname in SEMNAMES:
            sems[sname] = E(nc.semaphore(sname))

        block = E(nc.Block())
        finals = {}

        def run_for(me):
            def f(eng):
                em = Emitter(nc, me, eng, sems, sb, ps, ap, taps, upto)
                finals[me] = em.run()
            return f

        block.sync(run_for("sy"))
        block.gpsimd(run_for("gp"))
        block.tensor(run_for("pe"))
        block.scalar(run_for("act"))
        block.vector(run_for("dve"))

        ref = finals["sy"]
        for k, v in finals.items():
            assert v == ref, f"counter mismatch {k}: " + str(
                {kk: (v[kk], ref[kk]) for kk in v if v[kk] != ref[kk]})

    return nc, ap


def _stage_inputs(x, Wq, Wk, Wv, Wo, w1, w2, w3):
    f32 = np.float32
    x = np.ascontiguousarray(x.reshape(T, D), dtype=f32)

    af = (f32(1.0 / 1024.0) ** np.linspace(0, 1, 32, dtype=f32)).astype(f32)
    theta = np.arange(T, dtype=f32)[:, None] * af[None, :]
    cosT = np.cos(theta).astype(f32).T
    sinT = np.sin(theta).astype(f32).T
    cosb = np.ascontiguousarray(cosT.reshape(32, 2, 4096).transpose(1, 0, 2)
                                .reshape(64, 4096))
    sinb = np.ascontiguousarray(sinT.reshape(32, 2, 4096).transpose(1, 0, 2)
                                .reshape(64, 4096))
    ident = np.eye(128, dtype=f32)
    masks = np.zeros((128, 2048), dtype=f32)
    kk = np.arange(128)[:, None]
    qq = np.arange(512)[None, :]
    for b in range(4):
        masks[:, 512 * b:512 * b + 512] = (128 * b + kk <= qq).astype(f32)
    nonce = np.full((1, 8), np.random.randint(1, 2 ** 30), np.int32)

    wq_b = np.ascontiguousarray(
        Wq.reshape(16, 128, 16, 128).transpose(0, 3, 2, 1).reshape(16, 128, 2048))
    wo_b = np.ascontiguousarray(
        Wo.reshape(16, 128, 16, 128).transpose(0, 3, 2, 1).reshape(16, 128, 2048))
    w3r = w3.reshape(16, 128, 44, 128)

    maps = []
    for r in range(NCORE):
        w = r % 2
        rows = np.concatenate([x[512 * r:512 * r + 512],
                               x[512 * (15 - r):512 * (16 - r)]], axis=0)
        pos = np.concatenate([np.arange(512 * r, 512 * r + 512),
                              np.arange(512 * (15 - r), 512 * (16 - r))])
        kh = np.ascontiguousarray(Wk[1024 * w:1024 * w + 1024].T
                                  .reshape(16, 128, 1024))
        vh = np.ascontiguousarray(Wv[1024 * w:1024 * w + 1024].T
                                  .reshape(16, 128, 1024))
        wkvTb = np.ascontiguousarray(np.concatenate([kh, vh], axis=2))
        wF = w1 if w == 0 else w2
        wFb = np.ascontiguousarray(
            wF.reshape(44, 128, 16, 128).transpose(0, 3, 2, 1)
            .reshape(44, 128, 2048))
        w3b = np.ascontiguousarray(
            w3r[:, :, 22 * w:22 * w + 22, :].transpose(0, 3, 2, 1)
            .reshape(16, 128, 22 * 128))
        maps.append({
            "x_all": x,
            "x_own": np.ascontiguousarray(rows),
            "wkvTb": wkvTb,
            "wqTb": wq_b,
            "woTb": wo_b,
            "wFb": wFb,
            "w3b": w3b,
            "cosb": cosb,
            "sinb": sinb,
            "cos_own": np.ascontiguousarray(cosT[:, pos]),
            "sin_own": np.ascontiguousarray(sinT[:, pos]),
            "ident32": ident,
            "masks": masks,
            "nonce": nonce,
        })
    return maps


def kernel(L0_attn_x, L0_attn_Wq, L0_attn_Wk, L0_attn_Wv, L0_attn_Wo,
           L0_ffn_w1, L0_ffn_w2, L0_ffn_w3, **_):
    maps = _stage_inputs(np.asarray(L0_attn_x), np.asarray(L0_attn_Wq),
                         np.asarray(L0_attn_Wk), np.asarray(L0_attn_Wv),
                         np.asarray(L0_attn_Wo), np.asarray(L0_ffn_w1),
                         np.asarray(L0_ffn_w2), np.asarray(L0_ffn_w3))
    nc, _ap = build_nc()
    res = run_bass_kernel_spmd(nc, maps, core_ids=list(range(NCORE)))
    out = np.empty((T, D), np.float32)
    for r in range(NCORE):
        oT = np.asarray(res.results[r]["outT"], np.float32)
        out[512 * r:512 * r + 512] = oT[:, 0:512].T
        out[512 * (15 - r):512 * (16 - r)] = oT[:, 512:1024].T
    return out.reshape(1, T, D)
